# version 05 (final)
# baseline (speedup 1.0000x reference)
import numpy as np

import concourse.bass as bass
import concourse.mybir as mybir
from concourse.bass_utils import run_bass_kernel_spmd
from concourse.alu_op_type import AluOpType as OP

AF = mybir.ActivationFunctionType
F32 = mybir.dt.float32

NCORES = 8
D = 1024
KC = 8
TC = 4096
NT = 8
TT = 512
H = 16
HD = 64
CH = 128
NCH = TC // CH
EPS_LN = 1e-5
GROUPS = [[0, 1, 2, 3], [4, 5, 6, 7]]

DEBUG_CHAIN_BARRIER = True
BUILD_PHASES = 2


class Chan:
    def __init__(self, nc, name, step=1):
        self.sem = nc.semaphore(name).__enter__()
        self.count = 0
        self.step = step

    def tag(self, instr, n=None):
        n = self.step if n is None else n
        instr.then_inc(self.sem, n)
        self.count += n
        return self.count

    def wait(self, eng, token=None):
        t = self.count if token is None else token
        if t > 0:
            eng.wait_ge(self.sem, t)

    def dtag(self, eng, n=None):
        n = self.step if n is None else n
        eng.drain().then_inc(self.sem, n)
        self.count += n
        return self.count


class K:

    def __init__(self):
        nc = bass.Bass("TRN2", target_bir_lowering=False)
        nc.detect_race_conditions = False
        self.nc = nc
        self.pe = Chan(nc, "c_pe")
        self.dve = Chan(nc, "c_dve")
        self.act = Chan(nc, "c_act")
        self.gp = Chan(nc, "c_gp")
        self.ds = Chan(nc, "c_ds", 16)
        self.dg = Chan(nc, "c_dg", 16)
        self.cc = Chan(nc, "c_cc")
        self.chans = [self.pe, self.dve, self.act, self.gp, self.ds, self.dg, self.cc]
        self._alloc()

    def barrier(self):
        nc = self.nc
        self.dve.dtag(nc.vector)
        self.act.dtag(nc.scalar)
        self.gp.dtag(nc.gpsimd)
        self.pe.dtag(nc.tensor)
        for en in ("tensor", "vector", "scalar", "gpsimd", "sync"):
            eng = getattr(nc, en)
            for ch in self.chans:
                ch.wait(eng)

    def _alloc(self):
        nc = self.nc
        self.sbufs = []

        def sb(name, shape):
            t = nc.sbuf_tensor(name, shape, F32)
            t = t.__enter__()
            self.sbufs.append(t)
            return t

        def dr(name, shape, kind="Internal"):
            return nc.dram_tensor(name, shape, F32, kind=kind).ap()

        self.x_in = dr("x", [KC, 128, TC], kind="ExternalInput")
        self.y_out = dr("y", [KC, 128, TC], kind="ExternalOutput")
        self.w = {}
        for n in ["w0r", "w0k", "w0v", "q0w", "k0w", "v0w", "o0w",
                  "q1w", "k1w", "v1w", "o1w", "g1w"]:
            self.w[n] = dr(n, [KC, 128, D], kind="ExternalInput")
        self.w["e1w"] = dr("e1w", [8, KC, 128, D], kind="ExternalInput")
        self.sin = {}
        for n, shape in [
            ("bet0w", [KC, 128, H]), ("f0w", [KC, 128, H]),
            ("mu_r", [128, KC]), ("mu_k", [128, KC]), ("nw0", [128, KC]),
            ("b1q", [128, KC]), ("b1k", [128, KC]), ("b1v", [128, KC]),
            ("b1o", [128, KC]), ("bg1", [128, KC]),
            ("be1", [8, D]), ("be1c", [128, 8, KC]),
            ("ident", [128, 128]), ("onesrow", [1, 128]), ("onescol", [128, 1]),
            ("bd16", [KC, 128, H]), ("sel16", [KC, H, 128]),
            ("iota", [128, D]),
            ("halosel", [4, 1]), ("ohpref", [64, 4]),
            ("mskA", [128, 128]), ("mskB", [128, 128]), ("mskC", [128, 128]),
        ]:
            self.sin[n] = dr("i_" + n, shape, kind="ExternalInput")

        self.d_q = dr("d_q", [NCH, KC, 128, CH])
        self.d_k = dr("d_k", [NCH, KC, 128, CH])
        self.d_v = dr("d_v", [NCH, KC, 128, CH])
        self.d_qg = dr("d_qg", [NCH, H, HD, CH])
        self.d_beta = dr("d_beta", [NCH, H, CH])
        self.d_loga = dr("d_loga", [NCH, H, CH])
        self.d_AT = dr("d_AT", [NCH, H, HD, HD])
        self.d_BT = dr("d_BT", [NCH, H, HD, HD])
        self.d_Ob = dr("d_Ob", [NCH, H, CH, HD])
        self.d_PMT = dr("d_PMT", [NCH, H, HD, CH])
        self.d_gcol = dr("d_gcol", [NCH, 128, H])
        self.e_q = dr("e_q", [NT, KC, 128, TT])
        self.e_k = dr("e_k", [NT, KC, 128, TT])
        self.e_si = dr("e_si", [NT, H, TT])
        self.e_sa = dr("e_sa", [NT, H, TT])
        self.ag_a_in = dr("ag_a_in", [D, 1]); self.ag_a_out = dr("ag_a_out", [4 * D, 1])
        self.ag_c_in = dr("ag_c_in", [D, 1]); self.ag_c_out = dr("ag_c_out", [4 * D, 1])
        self.ag_b_in = dr("ag_b_in", [HD, 2 * H * HD])
        self.ag_b_out = dr("ag_b_out", [4 * HD, 2 * H * HD])
        self.ar1_in = dr("ar1_in", [128, 16]); self.ar1_out = dr("ar1_out", [128, 16])
        self.ar2_in = dr("ar2_in", [128, 16]); self.ar2_out = dr("ar2_out", [128, 16])
        self.ar3_in = dr("ar3_in", [128, 1025])
        self.ar3_out = dr("ar3_out", [128, 1025])

        self.xres = sb("xres", [128, KC, TC])
        self.wbuf = sb("wbuf", [128, 2, D])
        self.xh = sb("xh", [128, KC, TT + 1])
        self.rows = sb("rows", [1, 1536])
        self.rows16 = sb("rows16", [16, 1024])
        self.lastcol = sb("lastcol", [128, KC])
        self.bc = sb("bc", [128, 2, TT])
        self.mv = sb("mv", [128, 2, TT])
        self.arena = sb("arena", [128, 8192])
        self.sm = {}
        for n, shape in [
            ("mu_r", [128, KC]), ("mu_k", [128, KC]), ("nw0", [128, KC]),
            ("b1q", [128, KC]), ("b1k", [128, KC]), ("b1v", [128, KC]),
            ("b1o", [128, KC]), ("bg1", [128, KC]),
            ("bet0w", [128, KC, H]), ("f0w", [128, KC, H]),
            ("be1c", [128, 8, KC]),
            ("ident", [128, 128]), ("onesrow", [1, 128]), ("onescol", [128, 1]),
            ("bd16", [128, KC, H]),
            ("halosel", [4, 1]), ("ohpref", [64, 4]),
            ("mskA", [128, 128]), ("mskB", [128, 128]), ("mskC", [128, 128]),
        ]:
            self.sm[n] = sb("s_" + n, shape)
        self.cst = sb("cst", [128, 4])
        self.CONSTS = [1e-5, 1e-6, 1e-12, 2.0]
        for j, val in enumerate(self.CONSTS):
            nc.const_aps.register(F32, val, self.cst[:, j:j + 1])
        self.ps = []
        for i in range(8):
            t = nc.psum_tensor("psb%d" % i, [128, 512], F32).__enter__()
            self.ps.append(t)

    def bcast_row(self, ps_ap, row_ap, start=True, stop=True):
        return self.nc.tensor.matmul(ps_ap, self.sm["onesrow"][:1, :], row_ap,
                                     start=start, stop=stop)

    def colreduce(self, ps_ap, chunk_aps):
        n = len(chunk_aps)
        for i, a in enumerate(chunk_aps):
            ins = self.nc.tensor.matmul(ps_ap, self.sm["onescol"][:, :], a,
                                        start=(i == 0), stop=(i == n - 1))
        return ins

    def ph0_loads(self):
        nc = self.nc
        sy = nc.sync
        for j, val in enumerate(self.CONSTS):
            i = nc.vector.memset(self.cst[:, j:j + 1], val)
        nc.vector.memset(self.lastcol[:, :], 0.0)
        self.dve.dtag(nc.vector)
        for kc in range(KC):
            i = sy.dma_start(out=self.xres[:, kc, :], in_=self.x_in[kc])
            self.ds.tag(i)
        for n in ["mu_r", "mu_k", "nw0", "b1q", "b1k", "b1v", "b1o", "bg1",
                  "ident", "onesrow", "onescol",
                  "halosel", "ohpref", "mskA", "mskB", "mskC"]:
            i = sy.dma_start(out=self.sm[n][:], in_=self.sin[n])
            self.ds.tag(i)
        for n in ["bet0w", "f0w", "bd16"]:
            i = sy.dma_start(out=self.sm[n][:],
                             in_=self.sin[n].rearrange("a b c -> b a c"))
            self.ds.tag(i)
        i = sy.dma_start(out=self.sm["be1c"][:],
                         in_=self.sin["be1c"])
        self.ds.tag(i)

    def ln_stats(self, tile, xsrc=None):
        nc = self.nc
        v = nc.vector
        t0 = tile * TT
        chunks = ([self.xres[:, kc, t0:t0 + TT] for kc in range(KC)]
                  if xsrc is None else xsrc)
        ps_s, ps_q = self.ps[6], self.ps[3]
        sq = self.bc[:, 0, :]
        tv = self.dve.dtag(v)
        tg = self.gp.dtag(nc.gpsimd)
        self.dve.wait(nc.tensor, tv)
        self.gp.wait(nc.tensor, tg)
        self.act.wait(nc.tensor)
        self.dve.wait(nc.scalar, tv)
        self.gp.wait(nc.scalar, tg)
        tprev = None
        for kc in range(KC):
            if tprev is not None:
                self.pe.wait(nc.scalar, tprev)
            i = nc.scalar.activation(sq, chunks[kc], AF.Square)
            ta = self.act.dtag(nc.scalar)
            nc.tensor.matmul(ps_s[0:1, 0:TT], self.sm["onescol"][:, :], chunks[kc],
                             start=(kc == 0), stop=(kc == KC - 1))
            self.act.wait(nc.tensor, ta)
            i = nc.tensor.matmul(ps_q[0:1, 0:TT], self.sm["onescol"][:, :], sq,
                                 start=(kc == 0), stop=(kc == KC - 1))
            tprev = self.pe.tag(i)
        self.pe.wait(v, tprev)
        r = self.rows
        v.tensor_scalar(r[:1, 0:TT], ps_s[0:1, 0:TT], 1.0 / D, None, OP.mult)
        v.tensor_scalar(r[:1, 2 * TT:3 * TT], ps_q[0:1, 0:TT], 1.0 / D, None, OP.mult)
        v.tensor_tensor(r[:1, TT:2 * TT], r[:1, 0:TT], r[:1, 0:TT], OP.mult)
        v.nop()
        v.tensor_tensor(r[:1, TT:2 * TT], r[:1, 2 * TT:3 * TT],
                        r[:1, TT:2 * TT], OP.subtract)
        td = self.dve.dtag(v)
        self.dve.wait(nc.scalar, td)
        nc.scalar.activation(r[:1, 2 * TT:3 * TT], r[:1, TT:2 * TT], AF.Sqrt,
                             bias=EPS_LN)
        ta = self.act.dtag(nc.scalar)
        self.act.wait(v, ta)
        v.reciprocal(r[:1, TT:2 * TT], r[:1, 2 * TT:3 * TT])
        return self.dve.dtag(v)

    def xhat_tile(self, tile, halo_ag=None, dest=None):
        nc = self.nc
        v = nc.vector
        g = nc.gpsimd
        t0 = tile * TT
        dest = self.xh if dest is None else dest
        self.dve.wait(nc.tensor)
        self.bcast_row(self.ps[7][:, 0:TT], self.rows[:1, TT:2 * TT])
        i = self.bcast_row(self.ps[5][:, 0:TT], self.rows[:1, 0:TT])
        tp = self.pe.tag(i)
        self.pe.wait(v, tp)
        v.tensor_copy(self.bc[:, 1, :], self.ps[7][:, 0:TT])
        v.tensor_copy(self.bc[:, 0, :], self.ps[5][:, 0:TT])
        if halo_ag is None:
            for kc in range(KC):
                v.tensor_copy(dest[:, kc, 0:1], self.lastcol[:, kc:kc + 1])
        else:
            sy = nc.sync
            self.cc.wait(sy)
            i = sy.dma_start(out=self.rows16[0:4, 0:D],
                             in_=halo_ag.rearrange("(r f) a -> r (f a)", r=4))
            t = self.ds.tag(i)
            self.ds.wait(nc.tensor, t)
            for kc in range(KC):
                i = nc.tensor.matmul(self.ps[4][:, kc:kc + 1],
                                     self.rows16[0:4, kc * 128:(kc + 1) * 128],
                                     self.sm["halosel"][:, :], start=True, stop=True)
            th = self.pe.tag(i)
            self.pe.wait(v, th)
            for kc in range(KC):
                v.tensor_copy(dest[:, kc, 0:1], self.ps[4][:, kc:kc + 1])
        tb = self.dve.dtag(v)
        self.dve.wait(g, tb)
        xmu = self.mv
        for kc in range(KC):
            slot = kc % 2
            if kc >= 2:
                self.dve.wait(g)
            g.tensor_tensor(xmu[:, slot, :], self.xres[:, kc, t0:t0 + TT],
                            self.bc[:, 0, :], OP.subtract)
            tg = self.gp.dtag(g)
            self.gp.wait(v, tg)
            v.tensor_tensor(dest[:, kc, 1:TT + 1], xmu[:, slot, :],
                            self.bc[:, 1, :], OP.mult)
            self.dve.dtag(v)
        for kc in range(KC):
            v.tensor_copy(self.lastcol[:, kc:kc + 1], dest[:, kc, TT:TT + 1])
        return self.dve.dtag(v)

    def sweep(self, wname, moving_fn, epilogue_fn, eslot=None, first_mm_extra=None):
        nc = self.nc
        sy = nc.sync
        w = self.w[wname] if eslot is None else self.w[wname][eslot]
        self.pe.wait(sy)
        qtok = []
        for q in range(2):
            i = sy.dma_start(out=self.wbuf[:, q, :], in_=w[q])
            qtok.append(self.ds.tag(i))
        self.dve.wait(nc.tensor)
        self.act.wait(nc.tensor)
        self.gp.wait(nc.tensor)
        mm_last = None
        for kc in range(KC):
            slot = kc % 2
            self.ds.wait(nc.tensor, qtok[kc])
            mvap, mchan, mtok = moving_fn(kc)
            if mchan is not None:
                mchan.wait(nc.tensor, mtok)
            n = mvap.shape[-1]
            for oc in range(8):
                mm_last = nc.tensor.matmul(
                    self.ps[oc][0:128, 0:n],
                    self.wbuf[:, slot, oc * 128:(oc + 1) * 128],
                    mvap, start=(kc == 0), stop=(kc == KC - 1))
            t_q = self.pe.tag(mm_last)
            if kc + 2 < KC:
                self.pe.wait(sy, t_q)
                i = sy.dma_start(out=self.wbuf[:, slot, :], in_=w[kc + 2])
                qtok.append(self.ds.tag(i))
        t_all = self.pe.count
        for oc in range(8):
            epilogue_fn(oc, self.ps[oc], t_all)
        return t_all

    def next_weight_prefetch(self, wname, eslot=None):
        pass

    def l0ffn(self, halo_ag):
        nc = self.nc
        for tile in range(NT):
            self.ln_stats(tile)
            self.xhat_tile(tile, halo_ag if tile == 0 else None)
            self.l0ffn_tile(tile)
            self.barrier()

    def l0ffn_tile(self, tile):
        nc = self.nc
        v = nc.vector
        t0 = tile * TT
        rbuf = self.arena[:, 0:KC * TT].rearrange("p (kc t) -> p kc t", t=TT)
        kact = self.arena[:, KC * TT:2 * KC * TT].rearrange("p (kc t) -> p kc t", t=TT)

        def moving_shift(mu_name):
            mu = self.sm[mu_name]
            def fn(kc):
                slot = kc % 2
                self.pe.wait(v)
                v.tensor_tensor(self.mv[:, slot, :], self.xh[:, kc, 1:TT + 1],
                                self.xh[:, kc, 0:TT], OP.subtract)
                v.nop()
                v.scalar_tensor_tensor(self.mv[:, slot, :], self.mv[:, slot, :],
                                       mu[:, kc:kc + 1], self.xh[:, kc, 0:TT],
                                       OP.mult, OP.add)
                t = self.dve.dtag(v)
                return self.mv[:, slot, :], self.dve, t
            return fn

        def epi_r(oc, ps, tok):
            self.pe.wait(nc.scalar, tok)
            nc.scalar.activation(rbuf[:, oc, :], ps[:, 0:TT], AF.Sigmoid)
        self.sweep("w0r", moving_shift("mu_r"), epi_r)
        tr = self.act.dtag(nc.scalar)

        def epi_k(oc, ps, tok):
            self.pe.wait(v, tok)
            v.tensor_scalar(self.bc[:, oc % 2, :], ps[:, 0:TT], 0.0, None, OP.max)
            v.nop()
            v.tensor_tensor(kact[:, oc, :], self.bc[:, oc % 2, :],
                            self.bc[:, oc % 2, :], OP.mult)
        self.sweep("w0k", moving_shift("mu_k"), epi_k)
        tk = self.dve.dtag(v)

        def moving_kact(kc):
            return kact[:, kc, :], self.dve, tk
        tmp = self.mv
        def epi_v(oc, ps, tok):
            self.pe.wait(v, tok)
            self.act.wait(v, tr)
            v.tensor_tensor(tmp[:, oc % 2, :], ps[:, 0:TT], rbuf[:, oc, :], OP.mult)
            v.nop()
            v.scalar_tensor_tensor(self.xres[:, oc, t0:t0 + TT],
                                   tmp[:, oc % 2, :], 0.5,
                                   self.xres[:, oc, t0:t0 + TT], OP.mult, OP.add)
        self.sweep("w0v", moving_kact, epi_v)
        self.dve.dtag(v)

    def dn_passA(self):
        nc = self.nc
        v = nc.vector
        g = nc.gpsimd
        sy = nc.sync
        for kc in range(KC):
            i = sy.dma_start(
                out=self.arena[0:16, 5632 + kc * 128:5632 + (kc + 1) * 128],
                in_=self.sin["sel16"][kc])
            tsel = self.ds.tag(i)
        sel16 = self.arena[0:16, 5632:5632 + KC * 128].rearrange(
            "h (kc p) -> h kc p", p=128)
        qt = self.arena[:, 0:KC * TT].rearrange("p (kc t) -> p kc t", t=TT)
        arows = self.arena[0:16, KC * TT:KC * TT + 3 * TT]

        for tile in range(NT):
            self.ln_stats(tile)
            self.xhat_tile(tile)
            c0 = tile * 4

            def spill(dst4, tok_chan, tok):
                tok_chan.wait(g, tok)
                for kc in range(KC):
                    for j in range(4):
                        i = g.dma_start(out=dst4[c0 + j, kc],
                                        in_=qt[:, kc, j * CH:(j + 1) * CH])
                        self.dg.tag(i)

            def moving_xh(kc):
                return self.xh[:, kc, 1:TT + 1], self.dve, self.dve.count

            def epi_q(oc, ps, tok):
                self.pe.wait(nc.scalar, tok)
                nc.scalar.activation(qt[:, oc, :], ps[:, 0:TT], AF.Silu)
            self.dg.wait(nc.scalar)
            self.sweep("q0w", moving_xh, epi_q)
            tq = self.act.dtag(nc.scalar)
            spill(self.d_q, self.act, tq)
            self.dg.wait(nc.scalar)
            self.sweep("k0w", moving_xh, epi_q)
            tk = self.act.dtag(nc.scalar)
            self.act.wait(v, tk)
            tfree = self.dve.dtag(v)
            self.dve.wait(nc.tensor, tfree)
            ptoks = []
            for kc in range(KC):
                if kc >= 2:
                    self.pe.wait(v, ptoks[kc - 2])
                v.tensor_tensor(self.bc[:, kc % 2, :], qt[:, kc, :], qt[:, kc, :],
                                OP.mult)
                tb = self.dve.dtag(v)
                self.dve.wait(nc.tensor, tb)
                i = nc.tensor.matmul(self.ps[4][0:16, 0:TT],
                                     self.sm["bd16"][:, kc, :], self.bc[:, kc % 2, :],
                                     start=(kc == 0), stop=(kc == KC - 1))
                ptoks.append(self.pe.tag(i))
            self.pe.wait(nc.scalar, ptoks[-1])
            nc.scalar.activation(self.rows16[0:16, 0:TT],
                                 self.ps[4][0:16, 0:TT], AF.Sqrt, bias=1e-12)
            ta = self.act.dtag(nc.scalar)
            self.act.wait(v, ta)
            v.reciprocal(self.rows16[0:16, 0:TT], self.rows16[0:16, 0:TT])
            td = self.dve.dtag(v)
            self.dve.wait(nc.tensor, td)
            self.ds.wait(nc.tensor, tsel)
            seltoks = []
            for kc in range(KC):
                bank = self.ps[5] if kc % 2 == 0 else self.ps[2]
                if kc >= 2:
                    self.dve.wait(nc.tensor, seltoks[kc - 2])
                i = nc.tensor.matmul(bank[:, 0:TT], sel16[:, kc, :],
                                     self.rows16[0:16, 0:TT],
                                     start=True, stop=True)
                tp = self.pe.tag(i)
                self.pe.wait(v, tp)
                v.tensor_tensor(qt[:, kc, :], qt[:, kc, :], bank[:, 0:TT],
                                OP.mult)
                seltoks.append(self.dve.dtag(v))
            tk2 = self.dve.dtag(v)
            spill(self.d_k, self.dve, tk2)
            def epi_v(oc, ps, tok):
                self.pe.wait(v, tok)
                v.tensor_copy(qt[:, oc, :], ps[:, 0:TT])
            self.dg.wait(v)
            self.sweep("v0w", moving_xh, epi_v)
            tv2 = self.dve.dtag(v)
            spill(self.d_v, self.dve, tv2)
            tdve = self.dve.dtag(v)
            self.dve.wait(nc.tensor, tdve)
            self.act.wait(nc.tensor)
            for kc in range(KC):
                i = nc.tensor.matmul(self.ps[2][0:16, 0:TT],
                                     self.sm["bet0w"][:, kc, :],
                                     self.xh[:, kc, 1:TT + 1],
                                     start=(kc == 0), stop=(kc == KC - 1))
            tp1 = self.pe.tag(i)
            for kc in range(KC):
                i = nc.tensor.matmul(self.ps[1][0:16, 0:TT],
                                     self.sm["f0w"][:, kc, :],
                                     self.xh[:, kc, 1:TT + 1],
                                     start=(kc == 0), stop=(kc == KC - 1))
            tp2 = self.pe.tag(i)
            sc = nc.scalar
            self.dg.wait(sc)
            self.pe.wait(sc, tp1)
            sc.activation(arows[:, 0:TT], self.ps[2][0:16, 0:TT], AF.Sigmoid)
            self.pe.wait(sc, tp2)
            sc.activation(arows[:, TT:2 * TT], self.ps[1][0:16, 0:TT], AF.Sigmoid)
            sc.nop()
            sc.activation(arows[:, 2 * TT:3 * TT], arows[:, TT:2 * TT], AF.Ln)
            ta = self.act.dtag(sc)
            self.act.wait(v, ta)
            v.tensor_scalar(arows[:, 0:TT], arows[:, 0:TT], 2.0, None, OP.mult)
            tb2 = self.dve.dtag(v)
            self.dve.wait(g, tb2)
            for j in range(4):
                i = g.dma_start(out=self.d_beta[c0 + j],
                                in_=arows[:, j * CH:(j + 1) * CH])
                self.dg.tag(i)
                i = g.dma_start(out=self.d_loga[c0 + j],
                                in_=arows[:, 2 * TT + j * CH:2 * TT + (j + 1) * CH])
                self.dg.tag(i)
            self.barrier()
        self.barrier()

    def dn_passB(self):
        nc = self.nc
        v = nc.vector
        sc = nc.scalar
        g = nc.gpsimd
        sy = nc.sync
        A = self.arena
        cq = A[:, 0:1024].rearrange("p (kc t) -> p kc t", t=CH)
        ck = A[:, 1024:2048].rearrange("p (kc t) -> p kc t", t=CH)
        cv = A[:, 2048:3072].rearrange("p (kc t) -> p kc t", t=CH)
        scol = A[:, 3072:3088]
        nwcol = A[:, 3088:3104]
        gcol = A[:, 3104:3120]
        bgcol = A[:, 3120:3136]
        mel = A[:, 3152:3280]; mlt = A[:, 3280:3408]; mpt = A[:, 3408:3536]
        EL = A[:, 3664:3792]; ELT = A[:, 3792:3920]; EPT = A[:, 3920:4048]
        Lb = A[:, 4176:4304]; LTb = A[:, 4304:4432]; PTb = A[:, 4432:4560]
        ktm = A[:, 4560:4624]
        Lp = A[:, 4688:4816]; LpT = A[:, 4816:4944]
        Lp2 = A[:, 4944:5072]; Lp2T = A[:, 5072:5200]
        y0 = A[:, 5200:5328]; y1 = A[:, 5328:5456]
        wbar = A[:, 5456:5520]
        qg = A[:, 5520:5648]
        gc64 = A[:, 5648:5649]
        segP = A[0:64, 5712:6736]
        segPT = A[0:64, 6736:7760]
        segB = self.bc[:, :, :].rearrange("p a t -> p (a t)")[0:64, 0:1024]
        R16 = self.rows16
        brow = R16[0:16, 0:CH]
        larow = R16[0:16, CH:2 * CH]
        s0row = R16[0:16, 2 * CH:3 * CH]
        s1row = R16[0:16, 3 * CH:4 * CH]
        wrow = R16[0:16, 4 * CH:5 * CH]
        ident = self.sm["ident"]
        bufs = dict(cq=cq, ck=ck, cv=cv, scol=scol, nwcol=nwcol, gcol=gcol,
                    bgcol=bgcol, mel=mel, mlt=mlt, mpt=mpt, EL=EL, ELT=ELT,
                    EPT=EPT, Lb=Lb, LTb=LTb, PTb=PTb, ktm=ktm, Lp=Lp, LpT=LpT,
                    Lp2=Lp2, Lp2T=Lp2T, y0=y0, y1=y1, wbar=wbar, qg=qg,
                    gc64=gc64, segP=segP, segPT=segPT, segB=segB)
        tv = self.dve.dtag(v)
        self.dve.wait(g, tv)
        for h in range(H):
            v.tensor_copy(segP[:, h * 64:(h + 1) * 64], ident[0:64, 0:64])
            v.tensor_copy(segPT[:, h * 64:(h + 1) * 64], ident[0:64, 0:64])
        g.memset(segB[:, :], 0.0)
        self.gp.dtag(g)
        self.dve.dtag(v)

        for c in range(NCH):
            tve = self.dve.dtag(v)
            self.dve.wait(sy, tve)
            self.pe.wait(sy)
            for src, dst in [(self.d_q, cq), (self.d_k, ck), (self.d_v, cv)]:
                for kc in range(KC):
                    i = sy.dma_start(out=dst[:, kc, :], in_=src[c, kc])
                    self.ds.tag(i)
            i = sy.dma_start(out=brow, in_=self.d_beta[c])
            self.ds.tag(i)
            i = sy.dma_start(out=larow, in_=self.d_loga[c])
            tld = self.ds.tag(i)
            self.ds.wait(v, tld)
            cur, nxt = s0row, s1row
            v.tensor_copy(cur, larow)
            sh = 1
            while sh < CH:
                v.nop()
                v.tensor_copy(nxt[:, 0:sh], cur[:, 0:sh])
                v.tensor_tensor(nxt[:, sh:CH], cur[:, sh:CH], cur[:, 0:CH - sh],
                                OP.add)
                cur, nxt = nxt, cur
                sh *= 2
            srow = cur
            td = self.dve.dtag(v)
            self.dve.wait(sc, td)
            sc.activation(wrow, brow, AF.Ln)
            ta = self.act.dtag(sc)
            self.act.wait(v, ta)
            v.tensor_tensor(wrow, srow, wrow, OP.subtract)
            td = self.dve.dtag(v)
            self.dve.wait(nc.tensor, td)
            self.act.wait(nc.tensor)
            nc.tensor.transpose(self.ps[0][:, 0:16], srow, ident[0:16, 0:16])
            i = nc.tensor.transpose(self.ps[1][:, 0:16], wrow, ident[0:16, 0:16])
            tp = self.pe.tag(i)
            self.pe.wait(v, tp)
            v.tensor_copy(scol, self.ps[0][:, 0:16])
            v.nop()
            v.tensor_scalar(nwcol, self.ps[1][:, 0:16], -1.0, None, OP.mult)
            td = self.dve.dtag(v)
            self.dve.wait(sc, td)
            sc.activation(gcol, scol, AF.Exp)
            tshared0 = self.act.dtag(sc)
            self.act.wait(v, tshared0)
            v.tensor_scalar(s0row, wrow, srow[0:16, CH - 1:CH], None,
                            OP.subtract)
            v.nop()
            v.tensor_scalar(s0row, s0row, -1.0, None, OP.mult)
            td = self.dve.dtag(v)
            self.dve.wait(nc.tensor, td)
            i = nc.tensor.transpose(self.ps[2][:, 0:16], s0row, ident[0:16, 0:16])
            tp = self.pe.tag(i)
            self.pe.wait(sc, tp)
            sc.activation(bgcol, self.ps[2][:, 0:16], AF.Exp)
            tshared = self.act.dtag(sc)
            self.act.wait(v, tshared)
            self.act.wait(nc.tensor, tshared)
            for h in range(H):
                self.dn_unit(c, h, bufs)
        self.barrier()

    def dn_unit(self, c, h, B):
        nc = self.nc
        v = nc.vector
        sc = nc.scalar
        g = nc.gpsimd
        ident = self.sm["ident"]
        p0 = (h % 2) * 64
        kf = B["ck"][p0:p0 + 64, h // 2, :]
        qf = B["cq"][p0:p0 + 64, h // 2, :]
        vf = B["cv"][p0:p0 + 64, h // 2, :]
        mel, mlt, mpt = B["mel"], B["mlt"], B["mpt"]
        EL, ELT, EPT = B["EL"], B["ELT"], B["EPT"]
        Lb, LTb, PTb, ktm = B["Lb"], B["LTb"], B["PTb"], B["ktm"]
        Lp, LpT, Lp2, Lp2T = B["Lp"], B["LpT"], B["Lp2"], B["Lp2T"]
        y0, y1, wbar, qg, gc64 = B["y0"], B["y1"], B["wbar"], B["qg"], B["gc64"]
        scol, nwcol, gcol, bgcol = B["scol"], B["nwcol"], B["gcol"], B["bgcol"]

        self.dg.wait(v)
        self.dg.wait(sc)
        self.dve.wait(nc.tensor)
        self.act.wait(nc.tensor)
        nc.tensor.matmul(self.ps[0][0:1, 0:CH], scol[:, h:h + 1], ident[:, :],
                         start=True, stop=True)
        nc.tensor.matmul(self.ps[1][0:1, 0:CH], nwcol[:, h:h + 1], ident[:, :],
                         start=True, stop=True)
        i = nc.tensor.matmul(self.ps[2][0:1, 0:CH], gcol[:, h:h + 1], ident[:, :],
                             start=True, stop=True)
        tp = self.pe.tag(i)
        self.pe.wait(v, tp)
        rowS = mlt[0:1, 0:CH]
        rowW = mel[0:1, 0:CH]
        rowG = qg[0:1, 0:CH]
        v.tensor_copy(rowS, self.ps[0][0:1, 0:CH])
        v.tensor_copy(rowW, self.ps[1][0:1, 0:CH])
        v.tensor_copy(rowG, self.ps[2][0:1, 0:CH])
        td = self.dve.dtag(v)
        self.dve.wait(nc.tensor, td)
        self.bcast_row(self.ps[0][:, 0:CH], rowW)
        self.bcast_row(self.ps[1][:, 0:CH], rowS)
        nc.tensor.matmul(self.ps[2][:, 0:CH], kf, kf, start=True, stop=True)
        i = nc.tensor.matmul(self.ps[4][:, 0:CH], kf, qf, start=True, stop=True)
        tp = self.pe.tag(i)
        self.pe.wait(v, tp)
        v.tensor_tensor(mel, self.ps[0][:, 0:CH], self.sm["mskA"][:], OP.subtract)
        v.tensor_tensor(mlt, self.ps[1][:, 0:CH], self.sm["mskB"][:], OP.add)
        v.tensor_tensor(mpt, self.ps[1][:, 0:CH], self.sm["mskC"][:], OP.add)
        td = self.dve.dtag(v)
        self.dve.wait(sc, td)
        sc.activation(EL, mel, AF.Exp, bias=scol[:, h:h + 1], scale=1.0)
        sc.activation(ELT, mlt, AF.Exp, bias=nwcol[:, h:h + 1], scale=1.0)
        sc.activation(EPT, mpt, AF.Exp, bias=nwcol[:, h:h + 1], scale=1.0)
        ta = self.act.dtag(sc)
        self.act.wait(v, ta)
        v.tensor_tensor(Lb, self.ps[2][:, 0:CH], EL, OP.mult)
        v.tensor_tensor(LTb, self.ps[2][:, 0:CH], ELT, OP.mult)
        v.tensor_tensor(PTb, self.ps[4][:, 0:CH], EPT, OP.mult)
        td = self.dve.dtag(v)
        self.dve.wait(nc.tensor, td)
        idsl = ident[p0:p0 + 64, p0:p0 + 64]
        nc.tensor.transpose(self.ps[5][:, 0:64], kf, idsl)
        i = nc.tensor.transpose(self.ps[6][:, 0:64], vf, idsl)
        tp = self.pe.tag(i)
        self.pe.wait(v, tp)
        v.tensor_copy(ktm, self.ps[5][:, 0:64])
        v.tensor_copy(y0[:, 0:64], self.ps[6][:, 0:64])
        v.nop()
        v.tensor_scalar(y0[:, 64:128], ktm, gcol[:, h:h + 1], None, OP.mult)
        v.nop()
        v.tensor_scalar(wbar, ktm, bgcol[:, h:h + 1], None, OP.mult)
        tdy = self.dve.dtag(v)
        self.dve.wait(nc.tensor, tdy)
        i = nc.tensor.matmul(self.ps[7][:, 0:CH], LTb, y0[:, 0:CH],
                             start=True, stop=True)
        tp = self.pe.tag(i)
        self.pe.wait(v, tp)
        v.tensor_tensor(y1, y0, self.ps[7][:, 0:CH], OP.subtract)
        tdy = self.dve.dtag(v)
        ycur, ynxt = y1, y0
        lp, lpt = Lb, LTb
        pows = [(Lp2, Lp2T), (Lp, LpT)]
        for lev in range(1, 7):
            lp2, lp2t = pows[lev % 2]
            self.dve.wait(nc.tensor, tdy)
            nc.tensor.matmul(self.ps[5][:, 0:CH], lpt, lp, start=True, stop=True)
            i = nc.tensor.matmul(self.ps[6][:, 0:CH], lp, lpt, start=True, stop=True)
            tp = self.pe.tag(i)
            self.pe.wait(v, tp)
            v.tensor_copy(lp2, self.ps[5][:, 0:CH])
            v.tensor_copy(lp2t, self.ps[6][:, 0:CH])
            td = self.dve.dtag(v)
            self.dve.wait(nc.tensor, td)
            i = nc.tensor.matmul(self.ps[7][:, 0:CH], lp2t, ycur,
                                 start=True, stop=True)
            tp = self.pe.tag(i)
            self.pe.wait(v, tp)
            v.tensor_tensor(ynxt, ycur, self.ps[7][:, 0:CH], OP.add)
            tdy = self.dve.dtag(v)
            ycur, ynxt = ynxt, ycur
            lp, lpt = lp2, lp2t
        X = ycur
        Uv, M = X[:, 0:64], X[:, 64:128]
        self.dve.wait(nc.tensor, tdy)
        nc.tensor.matmul(self.ps[0][0:64, 0:64], M, wbar, start=True, stop=True)
        nc.tensor.matmul(self.ps[1][0:64, 0:64], Uv, wbar, start=True, stop=True)
        nc.tensor.matmul(self.ps[2][:, 0:64], PTb, Uv, start=True, stop=True)
        nc.tensor.matmul(self.ps[4][0:64, 0:CH], M, PTb, start=True, stop=True)
        nc.tensor.matmul(self.ps[5][:, 0:CH], self.sm["onesrow"][:1, 0:128],
                         rowG, start=True, stop=True)
        i = nc.tensor.matmul(self.ps[6][0:64, 0:1], self.sm["onesrow"][:1, 0:64],
                             rowG[0:1, CH - 1:CH], start=True, stop=True)
        tp = self.pe.tag(i)
        self.pe.wait(v, tp)
        AT = EL[0:64, 0:64]
        BT = EL[0:64, 64:128]
        v.tensor_copy(gc64[0:64, 0:1], self.ps[6][0:64, 0:1])
        v.tensor_copy(BT, self.ps[1][0:64, 0:64])
        v.tensor_copy(ELT[:, 0:64], self.ps[2][:, 0:64])
        v.scalar_tensor_tensor(AT, ident[0:64, 0:64], gc64[0:64, 0:1],
                               self.ps[0][0:64, 0:64], OP.mult, OP.subtract)
        self.pe.wait(sc, tp)
        sc.mul(EPT[0:64, 0:CH], self.ps[4][0:64, 0:CH], -1.0)
        v.tensor_tensor(qg[p0:p0 + 64, 0:CH], qf, self.ps[5][p0:p0 + 64, 0:CH],
                        OP.mult)
        td = self.dve.dtag(v)
        ta = self.act.dtag(sc)
        sP = B["segP"][:, h * 64:(h + 1) * 64]
        sPT = B["segPT"][:, h * 64:(h + 1) * 64]
        sB = B["segB"][:, h * 64:(h + 1) * 64]
        self.dve.wait(nc.tensor, td)
        nc.tensor.matmul(self.ps[0][0:64, 0:64], AT, sP, start=True, stop=True)
        nc.tensor.matmul(self.ps[1][0:64, 0:64], sP, AT, start=True, stop=True)
        nc.tensor.matmul(self.ps[2][0:64, 0:64], AT, sB, start=True, stop=False)
        i = nc.tensor.matmul(self.ps[2][0:64, 0:64], BT, ident[0:64, 0:64],
                             start=False, stop=True)
        tp = self.pe.tag(i)
        self.pe.wait(v, tp)
        v.tensor_copy(sP, self.ps[0][0:64, 0:64])
        v.tensor_copy(sPT, self.ps[1][0:64, 0:64])
        v.tensor_copy(sB, self.ps[2][0:64, 0:64])
        td2 = self.dve.dtag(v)
        self.dve.wait(g, td2)
        self.act.wait(g, ta)
        for dst, src in [(self.d_AT[c, h], AT), (self.d_BT[c, h], BT),
                         (self.d_Ob[c, h], ELT[:, 0:64]),
                         (self.d_PMT[c, h], EPT[0:64, 0:CH]),
                         (self.d_qg[c, h], qg[p0:p0 + 64, 0:CH])]:
            i = g.dma_start(out=dst, in_=src)
            self.dg.tag(i)

    def dn_passC(self, stop_after_prefix=False):
        nc = self.nc
        v = nc.vector
        sc = nc.scalar
        g = nc.gpsimd
        sy = nc.sync
        A = self.arena
        ident = self.sm["ident"]
        Sb = [A[0:64, 0:1024], A[0:64, 1024:2048]]
        Rbuf = A[0:64, 2048:3072]
        agbuf = A[0:64, 3072:5120]
        hs = A[:, 2048:5632]
        ATc = hs[0:64, 0:512].rearrange("p (h t) -> p h t", t=64)
        BTc = hs[0:64, 512:1024].rearrange("p (h t) -> p h t", t=64)
        Obc = hs[:, 1024:1536].rearrange("p (h t) -> p h t", t=64)
        PMTc = hs[0:64, 1536:2560].rearrange("p (h t) -> p h t", t=128)
        qgc = hs[0:64, 2560:3584].rearrange("p (h t) -> p h t", t=128)
        ostage = A[:, 5632:5760]
        sqb = A[:, 5760:5824]
        rcol = A[:, 5824:5828]
        ofeat = A[:, 6144:7168].rearrange("p (kc t) -> p kc t", t=CH)

        segPT = A[0:64, 6736:7760]
        segB = self.bc[:, :, :].rearrange("p a t -> p (a t)")[0:64, 0:1024]
        tve = self.dve.dtag(v)
        self.dve.wait(g, tve)
        i = g.dma_start(out=self.ag_b_in[:, 0:1024], in_=segPT)
        self.dg.tag(i)
        i = g.dma_start(out=self.ag_b_in[:, 1024:2048], in_=segB)
        self.dg.tag(i)
        self.dg.wait(g)
        i = g.collective_compute("AllGather", OP.bypass, replica_groups=GROUPS,
                                 ins=[self.ag_b_in.opt()], outs=[self.ag_b_out.opt()])
        tcc = self.cc.tag(i)
        self.dve.wait(g, tve)
        g.memset(Sb[0][:, :], 0.0)
        g.memset(Rbuf[:, :], 0.0)
        tg = self.gp.dtag(g)
        self.gp.wait(v, tg)
        self.gp.wait(sy, tg)
        self.cc.wait(sy, tcc)
        for j in range(4):
            self.pe.wait(sy)
            i = sy.dma_start(out=agbuf, in_=self.ag_b_out[j * 64:(j + 1) * 64, :])
            tld = self.ds.tag(i)
            self.ds.wait(v, tld)
            for h in range(H):
                v.scalar_tensor_tensor(Sb[0][:, h * 64:(h + 1) * 64],
                                       Rbuf[:, h * 64:(h + 1) * 64],
                                       self.sm["ohpref"][:, j:j + 1],
                                       Sb[0][:, h * 64:(h + 1) * 64],
                                       OP.mult, OP.add)
            td = self.dve.dtag(v)
            if j == 3:
                break
            self.dve.wait(nc.tensor, td)
            ptoks = {}
            for h in range(H):
                bank = h % 4
                if h >= 4:
                    self.dve.wait(nc.tensor)
                nc.tensor.matmul(self.ps[bank][0:64, 0:64],
                                 agbuf[:, h * 64:(h + 1) * 64],
                                 Rbuf[:, h * 64:(h + 1) * 64],
                                 start=True, stop=False)
                i = nc.tensor.matmul(self.ps[bank][0:64, 0:64], ident[0:64, 0:64],
                                     agbuf[:, 1024 + h * 64:1024 + (h + 1) * 64],
                                     start=False, stop=True)
                tp = self.pe.tag(i)
                self.pe.wait(v, tp)
                v.tensor_copy(Rbuf[:, h * 64:(h + 1) * 64], self.ps[bank][0:64, 0:64])
                self.dve.dtag(v)
        tpre = self.dve.dtag(v)
        if stop_after_prefix:
            self.dve.wait(v, tpre)
            v.tensor_copy(self.xres[0:64, 0, 0:1024], Sb[0][:, :])
            v.nop()
            v.tensor_copy(self.xres[0:64, 1, 0:1024], Rbuf[:, :])
            self.dve.dtag(v)
            return
        self.dve.wait(sy, tpre)

        nw = self.sm["nw0"]
        for c in range(NCH):
            Scur, Snxt = Sb[c % 2], Sb[(c + 1) % 2]
            for half in range(2):
                h0 = half * 8
                tve = self.dve.dtag(v)
                self.dve.wait(sy, tve)
                self.pe.wait(sy)
                for src, dst, wdt in [
                    (self.d_AT, ATc, 64), (self.d_BT, BTc, 64),
                    (self.d_Ob, Obc, 64), (self.d_PMT, PMTc, 128),
                    (self.d_qg, qgc, 128),
                ]:
                    i = sy.dma_start(
                        out=dst[:, 0:8, :],
                        in_=src[c, h0:h0 + 8].rearrange("h p t -> p h t"))
                    self.ds.tag(i)
                self.ds.wait(nc.tensor)
                if DEBUG_CHAIN_BARRIER:
                    self.barrier()
                for hh in range(8):
                    h = h0 + hh
                    slot = hh % 2
                    sS = Scur[:, h * 64:(h + 1) * 64]
                    sN = Snxt[:, h * 64:(h + 1) * 64]
                    ocols = ostage[:, slot * 64:(slot + 1) * 64]
                    self.dve.wait(nc.tensor)
                    nc.tensor.matmul(self.ps[4 + slot][:, 0:64], qgc[:, hh, :], sS,
                                     start=True, stop=False)
                    i = nc.tensor.matmul(self.ps[4 + slot][:, 0:64], PMTc[:, hh, :],
                                         sS, start=False, stop=True)
                    nc.tensor.matmul(self.ps[6 + slot][0:64, 0:64], ATc[:, hh, :],
                                     sS, start=True, stop=False)
                    i2 = nc.tensor.matmul(self.ps[6 + slot][0:64, 0:64],
                                          BTc[:, hh, :], ident[0:64, 0:64],
                                          start=False, stop=True)
                    tp = self.pe.tag(i2)
                    self.pe.wait(v, tp)
                    v.tensor_copy(sN, self.ps[6 + slot][0:64, 0:64])
                    v.tensor_tensor(ocols, self.ps[4 + slot][:, 0:64],
                                    Obc[:, hh, :], OP.add)
                    v.nop()
                    v.tensor_tensor(sqb[:, 0:64], ocols, ocols, OP.mult)
                    v.nop()
                    v.tensor_reduce(rcol[:, 0:1], sqb[:, 0:64],
                                    mybir.AxisListType.X, OP.add)
                    td = self.dve.dtag(v)
                    self.dve.wait(sc, td)
                    sc.activation(rcol[:, 1:2], rcol[:, 0:1], AF.Sqrt,
                                  bias=1e-6, scale=1.0 / HD)
                    ta = self.act.dtag(sc)
                    self.act.wait(v, ta)
                    v.reciprocal(rcol[:, 2:3], rcol[:, 1:2])
                    v.nop()
                    v.tensor_scalar(ocols, ocols, rcol[:, 2:3], None, OP.mult)
                    tdo = self.dve.dtag(v)
                    if h % 2 == 1:
                        self.dve.wait(nc.tensor, tdo)
                        i = nc.tensor.transpose(self.ps[2 + (h // 2) % 2][:, 0:CH],
                                                ostage, ident[:, :])
                        tp = self.pe.tag(i)
                        self.pe.wait(v, tp)
                        v.tensor_copy(ofeat[:, h // 2, :],
                                      self.ps[2 + (h // 2) % 2][:, 0:CH])
                        self.dve.dtag(v)
            if DEBUG_CHAIN_BARRIER:
                self.barrier()
            for kc in range(KC):
                v.tensor_scalar(ofeat[:, kc, :], ofeat[:, kc, :],
                                nw[:, kc:kc + 1], None, OP.mult)
            tof = self.dve.dtag(v)

            def moving_of(kc):
                return ofeat[:, kc, :], self.dve, tof
            t0 = c * CH
            tof2 = self.dve.dtag(v)
            self.dve.wait(nc.tensor, tof2)
            def epi_o(oc, ps, tok):
                self.pe.wait(v, tok)
                v.scalar_tensor_tensor(self.xres[:, oc, t0:t0 + CH],
                                       ps[:, 0:CH], 1.0,
                                       self.xres[:, oc, t0:t0 + CH],
                                       OP.mult, OP.add)
            self.sweep("o0w", moving_of, epi_o)
            self.dve.dtag(v)
            if DEBUG_CHAIN_BARRIER:
                self.barrier()
        self.barrier()

    def moe(self):
        nc = self.nc
        v = nc.vector
        sc = nc.scalar
        g = nc.gpsimd
        sy = nc.sync
        A = self.arena
        lg = A[:, 0:KC * TT].rearrange("p (kc t) -> p kc t", t=TT)
        ltm = A[:, 4096:5120]
        iota = A[:, 5120:6144]
        eq = A[:, 6144:7168]
        tq = A[:, 7168:8192]
        ident = self.sm["ident"]
        be1c = self.sm["be1c"]
        i = sy.dma_start(out=iota, in_=self.sin["iota"])
        tio = self.ds.tag(i)
        self.ds.wait(v, tio)
        for tile in range(NT):
            t0 = tile * TT
            self.ln_stats(tile)
            self.xhat_tile(tile)

            def moving_xh(kc):
                return self.xh[:, kc, 1:TT + 1], self.dve, self.dve.count
            bg = self.sm["bg1"]
            def epi_g(oc, ps, tok):
                self.pe.wait(v, tok)
                v.tensor_scalar(lg[:, oc, :], ps[:, 0:TT], bg[:, oc:oc + 1],
                                None, OP.add)
            self.sweep("g1w", moving_xh, epi_g)
            self.dve.dtag(v)
            mpc = self.bc[:, 0, :]
            for tb in range(4):
                for kc in range(KC):
                    self.dve.wait(nc.tensor)
                    i = nc.tensor.transpose(
                        self.ps[kc % 2][:, 0:128],
                        lg[:, kc, tb * 128:(tb + 1) * 128], ident[:, :])
                    tp = self.pe.tag(i)
                    self.pe.wait(v, tp)
                    v.tensor_copy(ltm[:, kc * 128:(kc + 1) * 128],
                                  self.ps[kc % 2][:, 0:128])
                    self.dve.dtag(v)
                cA = self.mv[:, 0, 0:16]
                v.tensor_reduce(cA[:, 0:1], ltm, mybir.AxisListType.X, OP.max)
                v.nop()
                v.tensor_scalar(cA[:, 1:2], cA[:, 0:1], -1.0, None, OP.mult)
                v.tensor_scalar(eq, ltm, cA[:, 0:1], None, OP.is_equal)
                v.tensor_scalar(tq, ltm, cA[:, 0:1], None, OP.is_lt)
                v.nop()
                v.tensor_tensor(eq, eq, iota, OP.mult)
                v.tensor_scalar(tq, tq, 1e9, None, OP.mult)
                v.nop()
                v.tensor_tensor(eq, eq, tq, OP.add)
                v.nop()
                v.tensor_reduce(cA[:, 2:3], eq, mybir.AxisListType.X, OP.min)
                v.nop()
                iA = self.mv[:, 1, 0:4].bitcast(mybir.dt.int32)
                v.tensor_copy(iA[:, 0:1], cA[:, 2:3])
                v.nop()
                v.tensor_scalar(iA[:, 1:2], iA[:, 0:1], 7, None, OP.bitwise_and)
                v.nop()
                v.tensor_copy(cA[:, 3:4], iA[:, 1:2])
                td = self.dve.dtag(v)
                self.dve.wait(sc, td)
                sc.activation(tq, ltm, AF.Exp, bias=cA[:, 1:2], scale=1.0)
                ta = self.act.dtag(sc)
                self.act.wait(v, ta)
                v.tensor_reduce(cA[:, 4:5], tq, mybir.AxisListType.X, OP.add)
                v.nop()
                v.reciprocal(cA[:, 5:6], cA[:, 4:5])
                v.nop()
                for e in range(8):
                    v.tensor_scalar(mpc[:, tb * 8 + e:tb * 8 + e + 1],
                                    cA[:, 3:4], float(e), None, OP.is_equal)
                v.nop()
                for e in range(8):
                    v.tensor_tensor(mpc[:, tb * 8 + e:tb * 8 + e + 1],
                                    mpc[:, tb * 8 + e:tb * 8 + e + 1],
                                    cA[:, 5:6], OP.mult)
                self.dve.dtag(v)
            rrow = self.rows16[0:1, 0:TT]
            for e in range(8):
                td = self.dve.dtag(v)
                self.dve.wait(nc.tensor, td)
                for tb in range(4):
                    self.dve.wait(nc.tensor)
                    i = nc.tensor.matmul(self.ps[2 + tb % 2][0:1, 0:128],
                                         mpc[:, tb * 8 + e:tb * 8 + e + 1],
                                         ident[:, :], start=True, stop=True)
                    tp = self.pe.tag(i)
                    self.pe.wait(v, tp)
                    v.tensor_copy(rrow[0:1, tb * 128:(tb + 1) * 128],
                                  self.ps[2 + tb % 2][0:1, 0:128])
                    self.dve.dtag(v)
                td = self.dve.dtag(v)
                self.dve.wait(nc.tensor, td)
                i = self.bcast_row(self.ps[4 + e % 2][:, 0:TT], rrow)
                tp = self.pe.tag(i)
                self.pe.wait(v, tp)
                v.tensor_copy(lg[:, e, :], self.ps[4 + e % 2][:, 0:TT])
                self.dve.dtag(v)
            tmpb = self.dve.dtag(v)
            self.pe.wait(sy)
            wtoks = []
            seq = [(e, kc) for e in range(8) for kc in range(KC)]
            for j in range(2):
                e, kc = seq[j]
                i = sy.dma_start(out=self.wbuf[:, j, :], in_=self.w["e1w"][e, kc])
                wtoks.append(self.ds.tag(i))
            self.dve.wait(nc.tensor, tmpb)
            self.act.wait(nc.tensor)
            self.gp.wait(nc.tensor)
            mm_last = None
            for j, (e, kc) in enumerate(seq):
                slot = j % 2
                self.ds.wait(nc.tensor, wtoks[j])
                self.pe.wait(v)
                v.tensor_tensor(self.mv[:, slot, :], self.xh[:, kc, 1:TT + 1],
                                lg[:, e, :], OP.mult)
                tm = self.dve.dtag(v)
                self.dve.wait(nc.tensor, tm)
                for oc in range(8):
                    mm_last = nc.tensor.matmul(
                        self.ps[oc][:, 0:TT],
                        self.wbuf[:, slot, oc * 128:(oc + 1) * 128],
                        self.mv[:, slot, :],
                        start=(j == 0), stop=(j == len(seq) - 1))
                tw = self.pe.tag(mm_last)
                if j + 2 < len(seq):
                    e2, kc2 = seq[j + 2]
                    self.pe.wait(sy, tw)
                    i = sy.dma_start(out=self.wbuf[:, slot, :],
                                     in_=self.w["e1w"][e2, kc2])
                    wtoks.append(self.ds.tag(i))
            t_all = self.pe.count
            for oc in range(8):
                self.pe.wait(v, t_all)
                v.tensor_scalar(self.bc[:, 1, :], lg[:, 0, :],
                                be1c[:, 0, oc:oc + 1], None, OP.mult)
                for e in range(1, 8):
                    v.nop()
                    v.scalar_tensor_tensor(self.bc[:, 1, :], lg[:, e, :],
                                           be1c[:, e, oc:oc + 1], self.bc[:, 1, :],
                                           OP.mult, OP.add)
                v.nop()
                v.tensor_tensor(self.bc[:, 1, :], self.ps[oc][:, 0:TT],
                                self.bc[:, 1, :], OP.add)
                td = self.dve.dtag(v)
                self.dve.wait(sc, td)
                sc.activation(self.bc[:, 1, :], self.bc[:, 1, :], AF.Relu)
                ta = self.act.dtag(sc)
                self.act.wait(v, ta)
                v.scalar_tensor_tensor(self.xres[:, oc, t0:t0 + TT],
                                       self.bc[:, 1, :], 0.5,
                                       self.xres[:, oc, t0:t0 + TT],
                                       OP.mult, OP.add)
                self.dve.dtag(v)
            self.barrier()
        self.barrier()

    def l1attn(self):
        nc = self.nc
        v = nc.vector
        sc = nc.scalar
        g = nc.gpsimd
        sy = nc.sync
        A = self.arena
        ident = self.sm["ident"]
        EPS = 1e-6
        L_FULL = 16384.0
        qt = A[:, 0:KC * TT].rearrange("p (kc t) -> p kc t", t=TT)
        kt = self.xh[:, :, 0:TT].rearrange("p kc t -> p kc t") if False else None
        ktv = self.xh
        sel_off = 5120
        sel16 = A[0:16, sel_off:sel_off + KC * 128].rearrange(
            "h (kc p) -> h kc p", p=128)
        sums = A[:, 6400:6448]
        R = self.rows16
        for kc in range(KC):
            i = sy.dma_start(out=A[0:16, sel_off + kc * 128:sel_off + (kc + 1) * 128],
                             in_=self.sin["sel16"][kc])
            tsel = self.ds.tag(i)
        self.ds.wait(nc.tensor, tsel)
        tz = self.dve.dtag(v)
        self.dve.wait(v, tz)
        v.memset(sums[:, 0:32], 0.0)
        self.dve.dtag(v)

        def bd_reduce(ps_ap, moving_fn, n):
            for kc in range(KC):
                ap, ch, tok = moving_fn(kc)
                if ch is not None:
                    ch.wait(nc.tensor, tok)
                i = nc.tensor.matmul(ps_ap, self.sm["bd16"][:, kc, :], ap,
                                     start=(kc == 0), stop=(kc == KC - 1))
            return self.pe.tag(i)

        for tile in range(NT):
            self.ln_stats(tile)
            self.xhat_tile(tile)
            def moving_xh(kc):
                return self.xh[:, kc, 1:TT + 1], self.dve, self.dve.count
            for wname, bias, dst, scol in [("q1w", "b1q", self.e_q, 0),
                                           ("k1w", "b1k", self.e_k, 8)]:
                bb = self.sm[bias]
                def epi(oc, ps, tok, bb=bb):
                    self.pe.wait(sc, tok)
                    sc.activation(qt[:, oc, :], ps[:, 0:TT], AF.Sigmoid,
                                  bias=bb[:, oc:oc + 1], scale=1.0)
                self.dg.wait(sc)
                self.sweep(wname, moving_xh, epi)
                ta = self.act.dtag(sc)
                self.act.wait(g, ta)
                for kc in range(KC):
                    i = g.dma_start(out=dst[tile, kc], in_=qt[:, kc, :])
                    self.dg.tag(i)
                self.act.wait(v, ta)
                for kc in range(KC):
                    v.tensor_reduce(self.mv[:, 0, 0:1], qt[:, kc, :],
                                    mybir.AxisListType.X, OP.add)
                    v.nop()
                    v.tensor_tensor(sums[:, scol + kc:scol + kc + 1],
                                    sums[:, scol + kc:scol + kc + 1],
                                    self.mv[:, 0, 0:1], OP.add)
                    v.nop()
                self.dve.dtag(v)
            self.barrier()
        td = self.dve.dtag(v)
        self.dve.wait(g, td)
        i = g.dma_start(out=self.ar1_in, in_=sums[:, 0:16])
        self.dg.tag(i)
        self.dg.wait(g)
        i = g.collective_compute("AllReduce", OP.add, replica_groups=GROUPS,
                                 ins=[self.ar1_in.opt()], outs=[self.ar1_out.opt()])
        tcc = self.cc.tag(i)
        self.cc.wait(sy, tcc)
        i = sy.dma_start(out=sums[:, 0:16], in_=self.ar1_out)
        t1 = self.ds.tag(i)
        self.ds.wait(v, t1)
        v.tensor_scalar(sums[:, 40:48], sums[:, 0:8], EPS, None, OP.add)
        v.tensor_scalar(sums[:, 32:40], sums[:, 8:16], EPS, None, OP.add)
        td = self.dve.dtag(v)
        self.dve.wait(nc.tensor, td)
        tk1 = bd_reduce(self.ps[6][0:16, 0:1],
                        lambda kc: (sums[:, 32:40][:, kc:kc + 1], None, 0), 8)
        tk2 = bd_reduce(self.ps[3][0:16, 0:1],
                        lambda kc: (sums[:, 40:48][:, kc:kc + 1], None, 0), 8)
        self.pe.wait(v, tk2)
        v.tensor_scalar(R[0:16, 0:1], self.ps[6][0:16, 0:1], EPS, None, OP.mult)
        v.tensor_scalar(R[0:16, 1:2], self.ps[3][0:16, 0:1], EPS, None, OP.mult)
        self.dve.dtag(v)

        for tile in range(NT):
            tdv = self.dve.dtag(v)
            self.dve.wait(sy, tdv)
            self.act.wait(sy)
            self.dg.wait(sy)
            self.pe.wait(sy)
            for kc in range(KC):
                i = sy.dma_start(out=qt[:, kc, :], in_=self.e_q[tile, kc])
                self.ds.tag(i)
            for kc in range(KC):
                i = sy.dma_start(out=ktv[:, kc, 0:TT], in_=self.e_k[tile, kc])
                self.ds.tag(i)
            tld = self.ds.tag(i, 0) if False else self.ds.count
            self.ds.wait(v, tld)
            self.ds.wait(nc.tensor, tld)
            for (src, secol, ccol, sirow, spill, qcol, acc_src) in [
                (qt, 32, 0, 0, self.e_si, 16, qt),
                (ktv, 40, 1, 2, None, 24, ktv),
            ]:
                self.dg.wait(v)
                def mv_dot(kc, src=src, secol=secol):
                    self.pe.wait(v)
                    xap = src[:, kc, :] if src is qt else src[:, kc, 0:TT]
                    v.tensor_scalar(self.mv[:, kc % 2, :], xap,
                                    sums[:, secol + kc:secol + kc + 1],
                                    None, OP.mult)
                    t = self.dve.dtag(v)
                    return self.mv[:, kc % 2, :], self.dve, t
                self.dve.wait(nc.tensor)
                self.act.wait(nc.tensor)
                tp = bd_reduce(self.ps[6][0:16, 0:TT], mv_dot, KC)
                self.pe.wait(v, tp)
                v.tensor_scalar(R[0:16, 512:512 + TT], self.ps[6][0:16, 0:TT],
                                R[0:16, ccol:ccol + 1], None, OP.add)
                v.nop()
                v.reciprocal(R[0:16, sirow * 256:sirow * 256 + 256]
                             if False else R[0:16, 512:512 + TT],
                             R[0:16, 512:512 + TT])
                tsi = self.dve.dtag(v)
                if spill is not None:
                    self.dve.wait(g, tsi)
                    i = g.dma_start(out=spill[tile], in_=R[0:16, 512:512 + TT])
                    self.dg.tag(i)
                for kc in range(KC):
                    self.dve.wait(nc.tensor, tsi)
                    i = nc.tensor.matmul(self.ps[2 + kc % 2][:, 0:TT],
                                         sel16[:, kc, :], R[0:16, 512:512 + TT],
                                         start=True, stop=True)
                    tp = self.pe.tag(i)
                    self.pe.wait(v, tp)
                    xap = src[:, kc, :] if src is qt else src[:, kc, 0:TT]
                    v.tensor_tensor(self.mv[:, kc % 2, :], xap,
                                    self.ps[2 + kc % 2][:, 0:TT], OP.mult)
                    v.nop()
                    v.tensor_reduce(self.bc[:, 0, 0:1], self.mv[:, kc % 2, :],
                                    mybir.AxisListType.X, OP.add)
                    v.nop()
                    v.tensor_tensor(sums[:, qcol + kc:qcol + kc + 1],
                                    sums[:, qcol + kc:qcol + kc + 1],
                                    self.bc[:, 0, 0:1], OP.add)
                    tsi = self.dve.dtag(v)
            self.barrier()
        td = self.dve.dtag(v)
        self.dve.wait(g, td)
        i = g.dma_start(out=self.ar2_in, in_=sums[:, 16:32])
        self.dg.tag(i)
        self.dg.wait(g)
        i = g.collective_compute("AllReduce", OP.add, replica_groups=GROUPS,
                                 ins=[self.ar2_in.opt()], outs=[self.ar2_out.opt()])
        tcc = self.cc.tag(i)
        self.cc.wait(sy, tcc)
        i = sy.dma_start(out=sums[:, 16:32], in_=self.ar2_out)
        t1 = self.ds.tag(i)
        self.ds.wait(v, t1)
        v.tensor_scalar(sums[:, 16:24], sums[:, 16:24], EPS, None, OP.add)
        v.tensor_scalar(sums[:, 24:32], sums[:, 24:32], EPS, None, OP.add)
        td = self.dve.dtag(v)
        self.dve.wait(nc.tensor, td)
        tk1 = bd_reduce(self.ps[6][0:16, 0:1],
                        lambda kc: (sums[:, 24:32][:, kc:kc + 1], None, 0), 8)
        tk2 = bd_reduce(self.ps[3][0:16, 0:1],
                        lambda kc: (sums[:, 16:24][:, kc:kc + 1], None, 0), 8)
        self.pe.wait(v, tk2)
        v.tensor_scalar(R[0:16, 0:1], self.ps[6][0:16, 0:1], EPS, None, OP.mult)
        v.tensor_scalar(R[0:16, 1:2], self.ps[3][0:16, 0:1], EPS, None, OP.mult)
        v.memset(R[0:16, 2:3], 0.0)
        self.dve.dtag(v)
        kvb = A[:, 4096:5120]
        tz = self.dve.dtag(v)
        self.dve.wait(v, tz)
        v.memset(kvb, 0.0)
        self.dve.dtag(v)

        for tile in range(NT):
            t0 = tile * TT
            self.ln_stats(tile)
            self.xhat_tile(tile)
            def moving_xh(kc):
                return self.xh[:, kc, 1:TT + 1], self.dve, self.dve.count
            bb = self.sm["b1v"]
            def epi_vv(oc, ps, tok):
                self.pe.wait(v, tok)
                v.tensor_scalar(qt[:, oc, :], ps[:, 0:TT], bb[:, oc:oc + 1],
                                None, OP.add)
            self.dg.wait(v)
            self.sweep("v1w", moving_xh, epi_vv)
            tvv = self.dve.dtag(v)
            self.dve.wait(sy, tvv)
            self.pe.wait(sy)
            for kc in range(KC):
                i = sy.dma_start(out=ktv[:, kc, 0:TT], in_=self.e_q[tile, kc])
                self.ds.tag(i)
            tld = self.ds.count
            self.ds.wait(v, tld)
            self.ds.wait(nc.tensor, tld)
            def mv_dot2(kc):
                self.pe.wait(v)
                v.tensor_scalar(self.mv[:, kc % 2, :], ktv[:, kc, 0:TT],
                                sums[:, 24 + kc:25 + kc], None, OP.mult)
                t = self.dve.dtag(v)
                return self.mv[:, kc % 2, :], self.dve, t
            self.dve.wait(nc.tensor)
            self.act.wait(nc.tensor)
            tp = bd_reduce(self.ps[6][0:16, 0:TT], mv_dot2, KC)
            self.pe.wait(v, tp)
            v.tensor_scalar(R[0:16, 512:512 + TT], self.ps[6][0:16, 0:TT],
                            R[0:16, 0:1], None, OP.add)
            td = self.dve.dtag(v)
            self.dve.wait(sc, td)
            sc.activation(R[0:16, 512:512 + TT], R[0:16, 512:512 + TT], AF.Sigmoid)
            ta = self.act.dtag(sc)
            self.act.wait(g, ta)
            i = g.dma_start(out=self.e_sa[tile], in_=R[0:16, 512:512 + TT])
            self.dg.tag(i)
            self.act.wait(sy, ta)
            tdv = self.dve.dtag(v)
            self.dve.wait(sy, tdv)
            for kc in range(KC):
                i = sy.dma_start(out=ktv[:, kc, 0:TT], in_=self.e_k[tile, kc])
                self.ds.tag(i)
            tld = self.ds.count
            self.ds.wait(v, tld)
            self.ds.wait(nc.tensor, tld)
            def mv_dot3(kc):
                self.pe.wait(v)
                v.tensor_scalar(self.mv[:, kc % 2, :], ktv[:, kc, 0:TT],
                                sums[:, 16 + kc:17 + kc], None, OP.mult)
                t = self.dve.dtag(v)
                return self.mv[:, kc % 2, :], self.dve, t
            tp = bd_reduce(self.ps[6][0:16, 0:TT], mv_dot3, KC)
            self.pe.wait(v, tp)
            self.dg.wait(v)
            v.tensor_scalar(R[0:16, 512:512 + TT], self.ps[6][0:16, 0:TT],
                            R[0:16, 1:2], None, OP.add)
            v.nop()
            v.tensor_scalar(R[0:16, 512:512 + TT], R[0:16, 512:512 + TT],
                            1.0, -1.0, OP.min, OP.max)
            td = self.dve.dtag(v)
            self.dve.wait(sc, td)
            sc.activation(R[0:16, 512:512 + TT], R[0:16, 512:512 + TT], AF.Exp)
            ta = self.act.dtag(sc)
            self.act.wait(v, ta)
            v.tensor_reduce(R[0:16, 3:4], R[0:16, 512:512 + TT],
                            mybir.AxisListType.X, OP.add)
            v.nop()
            v.tensor_tensor(R[0:16, 2:3], R[0:16, 2:3], R[0:16, 3:4], OP.add)
            tex = self.dve.dtag(v)
            for kc in range(KC):
                self.dve.wait(nc.tensor, tex)
                i = nc.tensor.matmul(self.ps[4][:, 0:TT], sel16[:, kc, :],
                                     R[0:16, 512:512 + TT], start=True, stop=True)
                tp = self.pe.tag(i)
                self.pe.wait(v, tp)
                v.tensor_tensor(self.mv[:, 0, :], qt[:, kc, :], self.ps[4][:, 0:TT],
                                OP.mult)
                tve = self.dve.dtag(v)
                self.dve.wait(nc.tensor, tve)
                for tb in range(4):
                    nc.tensor.transpose(self.ps[0][:, 0:128],
                                        ktv[:, kc, tb * 128:(tb + 1) * 128],
                                        ident[:, :])
                    i = nc.tensor.transpose(self.ps[1][:, 0:128],
                                            self.mv[:, 0, tb * 128:(tb + 1) * 128],
                                            ident[:, :])
                    tp = self.pe.tag(i)
                    self.pe.wait(v, tp)
                    v.tensor_copy(self.mv[:, 1, 0:128], self.ps[0][:, 0:128])
                    v.tensor_copy(self.bc[:, 1, 0:128], self.ps[1][:, 0:128])
                    tt = self.dve.dtag(v)
                    self.dve.wait(nc.tensor, tt)
                    i = nc.tensor.matmul(self.ps[2][:, 0:128],
                                         self.mv[:, 1, 0:128],
                                         self.bc[:, 1, 0:128],
                                         start=(tb == 0), stop=(tb == 3))
                tp = self.pe.tag(i)
                self.pe.wait(v, tp)
                v.tensor_tensor(kvb[:, kc * 128:(kc + 1) * 128],
                                kvb[:, kc * 128:(kc + 1) * 128],
                                self.ps[2][:, 0:128], OP.add)
                tex = self.dve.dtag(v)
            self.barrier()
        td = self.dve.dtag(v)
        self.dve.wait(g, td)
        i = g.dma_start(out=self.ar3_in[:, 0:1024], in_=kvb)
        self.dg.tag(i)
        v.memset(self.mv[:, 0, 0:1], 0.0)
        v.nop()
        v.tensor_copy(self.mv[0:16, 0, 0:1], R[0:16, 2:3])
        tse = self.dve.dtag(v)
        self.dve.wait(g, tse)
        i = g.dma_start(out=self.ar3_in[:, 1024:1025], in_=self.mv[:, 0, 0:1],
                        allow_slow_non_contiguous=True)
        self.dg.tag(i)
        self.dg.wait(g)
        i = g.collective_compute("AllReduce", OP.add, replica_groups=GROUPS,
                                 ins=[self.ar3_in.opt()], outs=[self.ar3_out.opt()])
        tcc = self.cc.tag(i)
        self.cc.wait(sy, tcc)
        i = sy.dma_start(out=kvb, in_=self.ar3_out[:, 0:1024])
        self.ds.tag(i)
        i = sy.dma_start(out=R[0:16, 2:3], in_=self.ar3_out[0:16, 1024:1025], allow_slow_non_contiguous=True)
        t1 = self.ds.tag(i)
        self.ds.wait(v, t1)
        v.reciprocal(R[0:16, 3:4], R[0:16, 2:3])
        v.nop()
        v.tensor_scalar(R[0:16, 3:4], R[0:16, 3:4], L_FULL, None, OP.mult)
        td = self.dve.dtag(v)
        self.dve.wait(nc.tensor, td)
        for kc in range(KC):
            i = nc.tensor.matmul(self.ps[4 + kc % 2][:, 0:1], sel16[:, kc, :],
                                 R[0:16, 3:4], start=True, stop=True)
            tp = self.pe.tag(i)
            self.pe.wait(v, tp)
            v.tensor_scalar(kvb[:, kc * 128:(kc + 1) * 128],
                            kvb[:, kc * 128:(kc + 1) * 128],
                            self.ps[4 + kc % 2][:, 0:1], None, OP.mult)
            self.dve.dtag(v)
        tkv = self.dve.dtag(v)

        ofeat = self.xh[:, :, 0:TT]
        for tile in range(NT):
            t0 = tile * TT
            tdv = self.dve.dtag(v)
            self.dve.wait(sy, tdv)
            self.pe.wait(sy)
            for kc in range(KC):
                i = sy.dma_start(out=qt[:, kc, :], in_=self.e_q[tile, kc])
                self.ds.tag(i)
            i = sy.dma_start(out=R[0:16, 0:TT], in_=self.e_si[tile])
            self.ds.tag(i)
            i = sy.dma_start(out=R[0:16, 512:512 + TT], in_=self.e_sa[tile])
            t1 = self.ds.tag(i)
            self.ds.wait(v, t1)
            self.ds.wait(nc.tensor, t1)
            v.tensor_tensor(R[0:16, 0:TT], R[0:16, 0:TT], R[0:16, 512:512 + TT],
                            OP.mult)
            tss = self.dve.dtag(v)
            for tb in range(4):
                self.dve.wait(nc.tensor, tss)
                i = nc.tensor.transpose(self.ps[6][:, 0:16],
                                        R[0:16, tb * 128:(tb + 1) * 128],
                                        ident[0:16, 0:16])
                tp = self.pe.tag(i)
                self.pe.wait(v, tp)
                v.tensor_copy(self.bc[:, 0, tb * 16:(tb + 1) * 16],
                              self.ps[6][:, 0:16])
                tss = self.dve.dtag(v)
            ostage = self.bc[:, 1, 0:128]
            for hp in range(8):
                for tb in range(4):
                    for half in range(2):
                        h = 2 * hp + half
                        p0 = half * 64
                        self.dve.wait(nc.tensor)
                        i = nc.tensor.matmul(
                            self.ps[half][:, 0:64],
                            qt[p0:p0 + 64, hp, tb * 128:(tb + 1) * 128],
                            kvb[p0:p0 + 64, hp * 128 + p0:hp * 128 + p0 + 64],
                            start=True, stop=True)
                        tp = self.pe.tag(i)
                        self.pe.wait(v, tp)
                        v.tensor_scalar(ostage[:, p0:p0 + 64],
                                        self.ps[half][:, 0:64],
                                        self.bc[:, 0, tb * 16 + h:tb * 16 + h + 1],
                                        None, OP.mult)
                        self.dve.dtag(v)
                    to = self.dve.dtag(v)
                    self.dve.wait(nc.tensor)
                    i = nc.tensor.transpose(self.ps[2 + tb % 2][:, 0:128],
                                            ostage, ident[:, :])
                    tp = self.pe.tag(i)
                    self.pe.wait(v, tp)
                    v.tensor_copy(ofeat[:, hp, tb * 128:(tb + 1) * 128],
                                  self.ps[2 + tb % 2][:, 0:128])
                    self.dve.dtag(v)
            tof = self.dve.dtag(v)
            def moving_of(kc):
                return ofeat[:, kc, :], self.dve, tof
            bo = self.sm["b1o"]
            def epi_o(oc, ps, tok):
                self.pe.wait(v, tok)
                v.scalar_tensor_tensor(self.xres[:, oc, t0:t0 + TT],
                                       ps[:, 0:TT], bo[:, oc:oc + 1],
                                       self.xres[:, oc, t0:t0 + TT],
                                       OP.add, OP.add)
            self.sweep("o1w", moving_of, epi_o)
            self.dve.dtag(v)
            self.barrier()
        self.barrier()

    def halo_ag(self, ag_in, ag_out):
        nc = self.nc
        v, sc, g, sy = nc.vector, nc.scalar, nc.gpsimd, nc.sync
        col = self.arena[:, 0:KC]
        tv = self.dve.dtag(v)
        tg = self.gp.dtag(g)
        self.dve.wait(sc, tv)
        self.gp.wait(sc, tg)
        for kc in range(KC):
            sc.activation(col[:, kc:kc + 1], self.xres[:, kc, TC - 1:TC], AF.Square)
        ta = self.act.dtag(sc)
        self.act.wait(nc.tensor, ta)
        self.dve.wait(nc.tensor, tv)
        self.colreduce(self.ps[0][0:1, 0:1],
                       [self.xres[:, kc, TC - 1:TC] for kc in range(KC)])
        i = self.colreduce(self.ps[2][0:1, 0:1], [col[:, kc:kc + 1] for kc in range(KC)])
        tp = self.pe.tag(i)
        self.pe.wait(v, tp)
        r = self.rows
        v.tensor_scalar(r[:1, 0:1], self.ps[0][0:1, 0:1], 1.0 / D, None, OP.mult)
        v.tensor_scalar(r[:1, 1:2], self.ps[2][0:1, 0:1], 1.0 / D, None, OP.mult)
        v.tensor_tensor(r[:1, 2:3], r[:1, 0:1], r[:1, 0:1], OP.mult)
        v.nop()
        v.tensor_tensor(r[:1, 1:2], r[:1, 1:2], r[:1, 2:3], OP.subtract)
        td = self.dve.dtag(v)
        self.dve.wait(sc, td)
        sc.activation(r[:1, 2:3], r[:1, 1:2], AF.Sqrt, bias=EPS_LN)
        ta = self.act.dtag(sc)
        self.act.wait(v, ta)
        v.reciprocal(r[:1, 1:2], r[:1, 2:3])
        td = self.dve.dtag(v)
        self.dve.wait(nc.tensor, td)
        self.bcast_row(self.ps[1][:, 0:1], r[:1, 0:1])
        i = self.bcast_row(self.ps[5][:, 0:1], r[:1, 1:2])
        tp = self.pe.tag(i)
        self.pe.wait(v, tp)
        for kc in range(KC):
            v.tensor_tensor(col[:, kc:kc + 1], self.xres[:, kc, TC - 1:TC],
                            self.ps[1][:, 0:1], OP.subtract)
        for kc in range(KC):
            v.tensor_tensor(col[:, kc:kc + 1], col[:, kc:kc + 1],
                            self.ps[5][:, 0:1], OP.mult)
        td = self.dve.dtag(v)
        self.dve.wait(g, td)
        agv = ag_in.rearrange("(kc p) a -> kc p a", p=128)
        for kc in range(KC):
            i = g.dma_start(out=agv[kc], in_=col[:, kc:kc + 1])
            self.dg.tag(i)
        self.dg.wait(g)
        i = g.collective_compute("AllGather", OP.bypass, replica_groups=GROUPS,
                                 ins=[ag_in.opt()], outs=[ag_out.opt()])
        self.cc.tag(i)

    def ph_out(self):
        nc = self.nc
        g = nc.gpsimd
        for kc in range(KC):
            i = g.dma_start(out=self.y_out[kc], in_=self.xres[:, kc, :])
            self.dg.tag(i)
        self.dg.wait(g)


def build(upto=99):
    k = K()
    k.ph0_loads()
    k.barrier()
    k.halo_ag(k.ag_a_in, k.ag_a_out)
    k.l0ffn(k.ag_a_out)
    k.barrier()
    if upto >= 2:
        k.dn_passA()
        k.dn_passB()
        k.dn_passC()
    if upto >= 3:
        k.halo_ag(k.ag_c_in, k.ag_c_out)
        k.l0ffn(k.ag_c_out)
        k.barrier()
    if upto >= 4:
        k.moe()
        k.barrier()
    if upto >= 5:
        k.l1attn()
        k.barrier()
    if upto >= 6:
        k.moe()
        k.barrier()
    k.ph_out()
    return k


def _fm(w):
    return np.ascontiguousarray(np.asarray(w, np.float32).reshape(KC, 128, -1))


def _pp(vec):
    return np.ascontiguousarray(np.asarray(vec, np.float32).reshape(KC, 128).T)


def build_host_inputs(inp):
    g = {}
    g["w0r"] = _fm(np.asarray(inp["L0_ffn_W_r"], np.float32).T)
    g["w0k"] = _fm(np.asarray(inp["L0_ffn_W_k"], np.float32).T)
    g["w0v"] = _fm(np.asarray(inp["L0_ffn_W_v"], np.float32).T)
    g["i_mu_r"] = _pp(inp["L0_ffn_mu_r"])
    g["i_mu_k"] = _pp(inp["L0_ffn_mu_k"])
    g["q0w"] = _fm(inp["L0_attn_Wq"]); g["k0w"] = _fm(inp["L0_attn_Wk"])
    g["v0w"] = _fm(inp["L0_attn_Wv"]); g["o0w"] = _fm(inp["L0_attn_Wo"])
    g["i_bet0w"] = _fm(inp["L0_attn_Wbet"]); g["i_f0w"] = _fm(inp["L0_attn_Wf"])
    g["i_nw0"] = _pp(inp["L0_attn_norm_w"])
    g["q1w"] = _fm(inp["L1_attn_Wq"]); g["k1w"] = _fm(inp["L1_attn_Wk"])
    g["v1w"] = _fm(inp["L1_attn_Wv"]); g["o1w"] = _fm(inp["L1_attn_Wo"])
    g["i_b1q"] = _pp(inp["L1_attn_bq"]); g["i_b1k"] = _pp(inp["L1_attn_bk"])
    g["i_b1v"] = _pp(inp["L1_attn_bv"]); g["i_b1o"] = _pp(inp["L1_attn_bo"])
    g["g1w"] = _fm(inp["L1_ffn_Wg"]); g["i_bg1"] = _pp(inp["L1_ffn_bg"])
    g["e1w"] = np.ascontiguousarray(
        np.stack([_fm(inp["L1_ffn_We"][e]) for e in range(8)]))
    g["i_be1"] = np.ascontiguousarray(np.asarray(inp["L1_ffn_be"], np.float32))
    g["i_be1c"] = np.ascontiguousarray(
        np.stack([_pp(inp["L1_ffn_be"][e]) for e in range(8)], axis=1))
    g["i_ident"] = np.eye(128, dtype=np.float32)
    g["i_onesrow"] = np.ones((1, 128), np.float32)
    g["i_onescol"] = np.ones((128, 1), np.float32)
    bd = np.zeros((KC, 128, H), np.float32)
    sel = np.zeros((KC, H, 128), np.float32)
    for kc in range(KC):
        for half in range(2):
            h = 2 * kc + half
            bd[kc, half * 64:(half + 1) * 64, h] = 1.0
            sel[kc, h, half * 64:(half + 1) * 64] = 1.0
    g["i_bd16"] = bd
    g["i_sel16"] = sel
    g["i_iota"] = np.ascontiguousarray(
        np.broadcast_to(np.arange(D, dtype=np.float32)[None, :], (128, D)))
    ii, jj = np.mgrid[0:128, 0:128]
    BIG = np.float32(1e30)
    g["i_mskA"] = np.where(jj >= ii, BIG, 0).astype(np.float32)
    g["i_mskB"] = np.where(jj <= ii, -BIG, 0).astype(np.float32)
    g["i_mskC"] = np.where(jj < ii, -BIG, 0).astype(np.float32)

    x = np.asarray(inp["L0_attn_x"], np.float32)
    per_core = []
    for core in range(NCORES):
        s, pos = core // 4, core % 4
        d = dict(g)
        xc = x[s, pos * TC:(pos + 1) * TC]
        d["x"] = np.ascontiguousarray(xc.T.reshape(KC, 128, TC))
        hs = np.zeros((4, 1), np.float32)
        if pos > 0:
            hs[pos - 1, 0] = 1.0
        d["i_halosel"] = hs
        ohp = np.zeros((64, 4), np.float32)
        ohp[:, pos] = 1.0
        d["i_ohpref"] = ohp
        per_core.append(d)
    return per_core


def kernel(**inputs):
    per_core = build_host_inputs(inputs)
    k = build(99)
    res = run_bass_kernel_spmd(k.nc, per_core, core_ids=list(range(NCORES)))
    out = np.zeros((2, 16384, 1024), np.float32)
    for core in range(NCORES):
        s, pos = core // 4, core % 4
        y = res.results[core]["y"]
        out[s, pos * TC:(pos + 1) * TC] = y.reshape(D, TC).T
    return out
